# version 13 (final)
# speedup vs baseline: 1.2818x; 1.0252x over previous
import contextlib
import math

import numpy as np
import ml_dtypes

import concourse.bass as bass
import concourse.mybir as mybir
from concourse.bass_utils import run_bass_kernel_spmd

F32 = mybir.dt.float32
BF16 = mybir.dt.bfloat16
I32 = mybir.dt.int32
U32 = mybir.dt.uint32
AX = mybir.AxisListType.X
ALU = mybir.AluOpType
ACT = mybir.ActivationFunctionType

E = 2048
D = 128
BASE_ = 10000.0
RMS_EPS = 1e-6
LN_EPS = 1e-5
INTER = 5504
NLUT = 16
QSCALE2 = 1.0 / D
NEG_BIG = -1.0e30

PROJ_OF = 1024
ICH = INTER // 128
SLOTS = 6
W_GU = SLOTS * 128 * E
W_O = 256 * E
SHARD = 3 * W_GU + W_O
OFF_WG = 0
OFF_WU = W_GU
OFF_WD = 2 * W_GU
OFF_WO = 3 * W_GU
BP_GU = 1536
BP_D = 2048
NP_GU = (SLOTS * 2048) // BP_GU
NP_D = (SLOTS * 2048) // BP_D


class Arena:
    def __init__(self, nc, budget=None):
        self.nc = nc
        self.base = (nc.sbuf_base + 63) // 64 * 64
        self.off = 0
        self.budget = (nc.sbuf_top - self.base - 2048) if budget is None else budget
        self.marks = {}
        self.peak = 0

    def mark(self, name):
        self.marks[name] = self.off

    def reset(self, name):
        self.off = self.marks[name]

    def al(self, name, shape, dtype, align=64):
        per_part = int(np.prod(shape[1:])) * mybir.dt.size(dtype)
        self.off = (self.off + align - 1) // align * align
        t = self.nc.alloc_sbuf_tensor_at(name, list(shape), dtype,
                                         offset=self.base + self.off)
        self.off += per_part
        self.peak = max(self.peak, self.off)
        assert self.off <= self.budget, f"SBUF arena overflow at {name}: {self.off}"
        return t


class MockInst:
    __slots__ = ()

    def then_inc(self, sem, n=1):
        return self

    def __enter__(self):
        return self

    def __exit__(self, *a):
        return False

    def __getattr__(self, k):
        return lambda *a, **kw: MockInst()


class MockEng:
    __slots__ = ()

    def __getattr__(self, k):
        return lambda *a, **kw: MockInst()


class P:
    def __init__(self, nc, es, record, events, sems):
        self.nc = nc
        self.record = record
        self.events = events
        self.counts = {}
        self.sems = sems
        self.es = es

    def sem(self, name):
        if not self.record and name not in self.sems:
            self.sems[name] = self.es.enter_context(self.nc.semaphore(name))
        self.counts[name] = 0
        return name

    def inc(self, inst, sem, n=1):
        self.counts[sem] += n
        if not self.record:
            inst.then_inc(self.sems[sem], n)
        return inst

    def ev(self, name, sem):
        if self.record:
            self.events[name] = (sem, self.counts[sem])

    def wait(self, eng, name):
        if self.record:
            return
        sem, val = self.events[name]
        if val > 0:
            eng.wait_ge(self.sems[sem], val)


def build_program(S=4096, debug=0, phases=1):
    nc = bass.Bass("TRN2", target_bir_lowering=False, detect_race_conditions=False)
    es = contextlib.ExitStack()

    NT = S // 128
    QT = 4
    NG = NT // QT
    TOK = S // 4

    din_, dout_ = {}, {}

    def din(name, shape, dtype):
        t = nc.dram_tensor(name, list(shape), dtype, kind="ExternalInput")
        din_[name] = (shape, dtype)
        return t

    def dout(name, shape, dtype):
        t = nc.dram_tensor(name, list(shape), dtype, kind="ExternalOutput")
        dout_[name] = (shape, dtype)
        return t

    xb = din("xb", (S, E), F32)
    x_rows = din("x_rows", (TOK, E), F32)
    wproj = din("wproj", (E, PROJ_OF), F32)
    cos_pk = din("cos_pk", (128, NT * 128), F32)
    sin_pk = din("sin_pk", (128, NT * 128), F32)
    wrow = din("wrow", (128, 640), F32)
    s1r = din("s1r", (128, 4), F32)
    eps_row = din("eps_row", (128, 3), F32)
    s2r = din("s2r", (128, 4), F32)
    gs_row = din("gs_row", (128, 4), F32)
    gs_rep = din("gs_rep", (128, 1), F32)
    s1f = din("s1f", (128, 1), F32)
    s2f = din("s2f", (128, 1), F32)
    mask_qk = din("mask_qk", (128, 128), F32)
    mask_kq = din("mask_kq", (128, 128), F32)
    ones_in = din("ones_in", (128, 128), BF16)
    ident_in = din("ident_in", (128, 128), F32)
    rank4 = din("rank4", (1, 1), U32)
    lut_rep = din("lut_rep", (128, NLUT), F32)
    g_idx = din("g_idx", (128, SLOTS * 2048), I32)
    u_idx = din("u_idx", (128, SLOTS * 2048), I32)
    d_idx = din("d_idx", (128, SLOTS * 2048), I32)
    wo_sl = din("wo_sl", (256, E), F32)

    o_rows = dout("o_rows", (TOK, E), F32)

    xln_d = nc.dram_tensor("xln_d", [S, E], BF16)
    q_d = nc.dram_tensor("q_d", [S, 512], BF16)
    k_d = nc.dram_tensor("k_d", [S, 128], BF16)
    rq_d = nc.dram_tensor("rq_d", [S, 128], BF16)
    rk_d = nc.dram_tensor("rk_d", [S, 128], BF16)
    nm_d = nc.dram_tensor("nm_d", [128, 128], BF16)
    bnd_d = nc.dram_tensor("bnd_d", [128, 288], BF16)
    bnd2_d = nc.dram_tensor("bnd2_d", [128, 16], BF16)
    xln2_d = nc.dram_tensor("xln2_d", [TOK, E], BF16)
    cc_attn_in = nc.dram_tensor("cc_attn_in", [512, S], BF16)
    cc_attn_out = nc.dram_tensor("cc_attn_out", [4, 512, S], BF16)
    cc_wg_in = nc.dram_tensor("cc_wg_in", [W_GU], BF16)
    cc_wg_out = nc.dram_tensor("cc_wg_out", [8, W_GU], BF16, addr_space="Shared")
    cc_wu_in = nc.dram_tensor("cc_wu_in", [W_GU], BF16)
    cc_wu_out = nc.dram_tensor("cc_wu_out", [8, W_GU], BF16, addr_space="Shared")
    cc_wd_in = nc.dram_tensor("cc_wd_in", [W_GU], BF16)
    cc_wd_out = nc.dram_tensor("cc_wd_out", [8, W_GU], BF16, addr_space="Shared")
    cc_wo_in = nc.dram_tensor("cc_wo_in", [W_O], BF16)
    cc_wo_out = nc.dram_tensor("cc_wo_out", [8, W_O], BF16, addr_space="Shared")

    if debug >= 1:
        dbg_q = dout("dbg_q", (S, 512), BF16)
        dbg_k = dout("dbg_k", (S, 128), BF16)
        dbg_rq = dout("dbg_rq", (S, 128), BF16)
        dbg_rk = dout("dbg_rk", (S, 128), BF16)
        dbg_v = dout("dbg_v", (128, S), BF16)
        dbg_xln = dout("dbg_xln", (S, E), BF16)
        dbg_wts = dout("dbg_wts", (SHARD,), BF16)
        dbg_xlt = dout("dbg_xlt", (128, 4 * 128 * 16), BF16)
        dbg_wt = dout("dbg_wt", (128, 16 * PROJ_OF), BF16)
        dbg_attn = dout("dbg_attn", (512, S), BF16)
        dbg_nm = dout("dbg_nm", (128, 4096), BF16)
        dbg_mxc = dout("dbg_mxc", (128, 128), F32)
        dbg_pb = dout("dbg_pb", (128, 512), BF16)
        dbg_rec = dout("dbg_rec", (1, 512), F32)
        dbg_rep = dout("dbg_rep", (128, 512), F32)
        dbg_ao = dout("dbg_ao", (128, 512), BF16)

    A = Arena(nc)
    c_wrow = A.al("c_wrow", [128, 640], F32)
    c_s1r = A.al("c_s1r", [128, 4], F32)
    c_eps = A.al("c_eps", [128, 3], F32)
    c_s2r = A.al("c_s2r", [128, 4], F32)
    c_gsr = A.al("c_gsr", [128, 4], F32)
    c_mqk = A.al("c_mqk", [128, 128], F32)
    c_mkq = A.al("c_mkq", [128, 128], F32)
    c_ones = A.al("c_ones", [128, 128], BF16)
    c_id32 = A.al("c_id32", [128, 128], F32)
    c_gs = A.al("c_gs", [128, 1], F32)
    c_s1f = A.al("c_s1f", [128, 1], F32)
    c_s2f = A.al("c_s2f", [128, 1], F32)
    s1vec = A.al("s1vec", [128, 1], F32)
    s2vec = A.al("s2vec", [128, 1], F32)
    c_lut = A.al("c_lut", [128, NLUT], F32)
    c_rank = A.al("c_rank", [1, 1], U32)
    c_ones32 = A.al("c_ones32", [128, 128], F32)
    A.mark("big")
    v_buf = A.al("v_buf", [128, S], BF16)
    bd_acc = A.al("bd_acc", [128, BP_D], BF16)
    bd_tmp = A.al("bd_tmp", [128, BP_D], BF16)
    mxc = A.al("mxc", [128, 128], F32)
    A_sb = A.al("A_sb", [128, 128], BF16)
    A1_sb = A.al("A1_sb", [128, 128], BF16)
    A2_sb = A.al("A2_sb", [128, 128], BF16)
    kcol = A.al("kcol", [128, 32], BF16)
    mx_bc = A.al("mx_bc", [128, 16], F32)
    gsa1 = A.al("gsa1", [128, 4], F32)
    gsa2 = A.al("gsa2", [128, 4], F32)
    maxes_bf = A.al("maxes_bf", [1, 16], BF16)
    wo_bf = A.al("wo_bf", [128, 1024], BF16)
    A.mark("ph")

    wt_bf = A.al("wt_bf", [128, 16 * PROJ_OF], BF16)
    c_full = A.al("c_full", [128, S], F32)
    s_full = A.al("s_full", [128, S], F32)
    xbuf = [A.al(f"xbuf{i}", [128, E], F32) for i in range(4)]
    xlnb = [A.al(f"xlnb{i}", [128, E], BF16) for i in range(4)]
    xlt = [A.al(f"xlt{i}", [128, QT * 128 * 16], BF16) for i in range(2)]
    qkf = [A.al(f"qkf{i}", [128, 640], F32) for i in range(QT)]
    qko = [A.al(f"qko{i}", [128, 640], BF16) for i in range(QT)]
    rqko = [A.al(f"rqko{i}", [128, 256], BF16) for i in range(QT)]
    r_u = A.al("r_u", [128, 512], F32)
    rs_b = A.al("rs_b", [128, 52], F32)
    st_scr = A.al("st_scr", [128, E // 2], F32)
    st_u = A.al("st_u", [128, 4], F32)
    st_m = A.al("st_m", [128, QT], F32)
    st_s = A.al("st_s", [128, QT], F32)
    st_t = A.al("st_t", [128, QT], F32)
    rs_a = [A.al(f"rs_a{i}", [128, 5], F32) for i in range(QT)]

    A.mark("ph1_end")
    A.reset("ph")
    qT = [A.al(f"qT{i}", [128, S], BF16) for i in range(4)]
    kT = A.al("kT", [128, S], BF16)
    RqT = A.al("RqT", [128, S], BF16)
    RkT = A.al("RkT", [128, S], BF16)
    Rq_s1 = A.al("Rq_s1", [128, S], BF16)
    Rk_s2 = A.al("Rk_s2", [128, S], BF16)
    negmaxT = A.al("negmaxT", [128, 4096], BF16)
    nm_sb = A.al("nm_sb", [128, 128], BF16)
    rd_all = A.al("rd_all", [128, 288], BF16)
    rdc = A.al("rdc", [128, 16], BF16)
    rd2 = A.al("rd2", [1, 1152], BF16)
    rec_sb = A.al("rec_sb", [1, 1024], F32)
    rep_sb = A.al("rep_sb", [128, 1024], F32)
    probsT = [A.al(f"probsT{i}", [128, 1024], BF16) for i in range(2)]
    ao_sb = [A.al(f"ao_sb{i}", [128, 1024], BF16) for i in range(4)]
    bd_i32 = [A.al(f"bd_i32_{i}", [128, BP_D], I32) for i in range(2)]
    bd_bf = [A.al(f"bd_bf_{i}", [128, BP_D], BF16) for i in range(2)]
    ph2_end = A.off
    if A.marks["ph1_end"] > ph2_end:
        A.off = A.marks["ph1_end"]

    A.mark("ph2_end2")
    A.reset("big")
    res1 = A.al("res1", [128, 16 * TOK], F32)
    wg_sb = [A.al(f"wg_sb{i}", [128, 2048], BF16) for i in range(2)]
    wu_sb = [A.al(f"wu_sb{i}", [128, 2048], BF16) for i in range(2)]
    wd_sb = [A.al(f"wd_sb{i}", [128, 1536], BF16) for i in range(2)]
    gt_sb = [A.al(f"gt_sb{i}", [128, TOK], BF16) for i in range(2)]
    mrep = A.al("mrep", [128, TOK], F32)
    rrep = A.al("rrep", [128, TOK], F32)
    mrow = A.al("mrow", [1, TOK], F32)
    rrow = A.al("rrow", [1, TOK], F32)
    vrow = A.al("vrow", [1, TOK], F32)
    A.mark("p3ov")
    at_sb = A.al("at_sb", [128, 16 * TOK], BF16)
    xr_sb = [A.al(f"xr_sb{i}", [128, E], F32) for i in range(2)]
    wo_half = A.al("wo_half", [128, 16 * 1024], BF16)
    p3_era1 = A.off
    A.reset("p3ov")
    NCH1 = len([j for j in range(ICH) if j % 8 < 4])
    NCH2 = ICH - NCH1
    xl2 = A.al("xl2", [128, 16 * TOK], BF16)
    h_sb = A.al("h_sb", [128, max(NCH1, NCH2) * TOK], BF16)
    sq_sb = [A.al(f"sq_sb{i}", [128, TOK], F32) for i in range(2)]
    orow = [A.al(f"orow{i}", [128, 512], F32) for i in range(2)]
    A.off = max(A.off, p3_era1)

    GRP = [[j for j in range(ICH) if j % 8 < 4], [j for j in range(ICH) if j % 8 >= 4]]
    NST = TOK // 128
    QHS = [(512 * i, min(512, TOK - 512 * i)) for i in range((TOK + 511) // 512)]

    PSB = [nc.alloc_psum_tensor(f"psb{i}", [128, 512], F32) for i in range(8)]

    def a_chunks(jt):
        nkeys = 128 * (jt + 1)
        return [(ck * 512, min(512, nkeys - ck * 512))
                for ck in range((nkeys + 511) // 512)]

    MATS = [("g", g_idx, 0, BP_GU, NP_GU), ("u", u_idx, 0, BP_GU, NP_GU),
            ("d", d_idx, 0, BP_D, NP_D)]
    MATS2 = [MATS[0] + (cc_wg_in,), MATS[1] + (cc_wu_in,), MATS[2] + (cc_wd_in,)]
    PIECES = [(m, pc, MATS[m][3], MATS[m][2], MATS2[m][5])
              for m in range(3) for pc in range(MATS[m][4])]

    events = {}
    sems = {}

    for record in (True, False):
        p = P(nc, es, record, events, sems)
        s_sp = p.sem("s_sp")
        s_gp = p.sem("s_gp")
        s_pe = p.sem("s_pe")
        s_v = p.sem("s_v")
        s_sc = p.sem("s_sc")
        s_cc = p.sem("s_cc")
        s_bl = p.sem("s_bl")
        s_bs = p.sem("s_bs")
        s_bnd = p.sem("s_bnd")
        s_consts = p.sem("s_consts")
        s_wt = p.sem("s_wt")
        s_tbl = p.sem("s_tbl")
        s_x4 = [p.sem(f"s_x{i}") for i in range(4)]
        s_xst4 = [p.sem(f"s_xst{i}") for i in range(4)]
        s_xlt2 = [p.sem(f"s_xlt{i}") for i in range(2)]
        s_qk2 = [p.sem(f"s_qk{i}") for i in range(2)]
        s_p2 = p.sem("s_p2")
        s_nm = p.sem("s_nm")
        s_ao2 = [p.sem(f"s_ao{i}") for i in range(2)]
        s_at = p.sem("s_at")
        s_xr2 = [p.sem(f"s_xr{i}") for i in range(2)]
        s_wo = p.sem("s_wo")
        s_gu2 = [p.sem(f"s_gu{i}") for i in range(2)]
        s_wd2 = [p.sem(f"s_wd{i}") for i in range(2)]
        s_or2 = [p.sem(f"s_or{i}") for i in range(2)]
        gu_hist = []
        wd_hist = []

        def emit_sync(sync):
            def _sy_piece_store(i):
                m2, pc2, bp2, base2, ctgt2 = PIECES[i]
                p.wait(sync, f"bacc_{m2}_{pc2}")
                dims2 = ([[768, 128], [768 * 128, 2], [1, 768]] if bp2 == BP_GU
                         else [[2048, 128], [1, 2048]])
                p.inc(sync.dma_start(out=bass.AP(ctgt2, base2 + pc2 * 128 * bp2, dims2),
                                     in_=bd_acc[:, 0:bp2]), s_bs, 16)
                p.ev(f"bst_{m2}_{pc2}", s_bs)

            def _sy_piece_load(i):
                m2, pc2, bp2, base2, ctgt2 = PIECES[i]
                if i >= 2:
                    pm2, ppc2 = PIECES[i - 2][0], PIECES[i - 2][1]
                    p.wait(sync, f"bcvt_{pm2}_{ppc2}")
                p.inc(sync.dma_start(out=bd_i32[i % 2][:, 0:bp2],
                                     in_=MATS[m2][1].ap()[:, pc2 * bp2:(pc2 + 1) * bp2]),
                      s_bl, 16)
                p.ev(f"bidx_{m2}_{pc2}", s_bl)

            for src, dst in [
                (wrow, c_wrow), (s1r, c_s1r), (s2r, c_s2r), (gs_row, c_gsr),
                (eps_row, c_eps),
                (mask_qk, c_mqk), (mask_kq, c_mkq), (ones_in, c_ones),
                (ident_in, c_id32), (gs_rep, c_gs), (s1f, c_s1f), (s2f, c_s2f),
                (lut_rep, c_lut), (rank4, c_rank),
            ]:
                p.inc(sync.dma_start(out=dst[:], in_=src.ap()), s_consts, 16)
            p.ev("consts", s_consts)

            p.inc(sync.dma_start(out=c_full[:], in_=cos_pk.ap()), s_tbl, 16)
            p.inc(sync.dma_start(out=s_full[:], in_=sin_pk.ap()), s_tbl, 16)
            p.ev("tblf", s_tbl)
            for gq in range(NG):
                for ti in range(QT):
                    t = gq * QT + ti
                    if gq >= 1:
                        p.wait(sync, f"xln{t - QT}")
                    p.inc(sync.dma_start(out=xbuf[ti][:],
                                         in_=xb.ap()[128 * t:128 * (t + 1), :]),
                          s_x4[ti], 16)
                    p.ev(f"xld{t}", s_x4[ti])
                for ti in range(QT):
                    t = gq * QT + ti
                    p.wait(sync, f"xln{t}")
                    p.inc(sync.dma_start(out=xln_d.ap()[128 * t:128 * (t + 1), :],
                                         in_=xlnb[ti][:]), s_xst4[ti], 16)
                    p.ev(f"xlnst{t}", s_xst4[ti])
                for ti in range(QT):
                    p.wait(sync, f"xlnst{gq * QT + ti}")
                if gq >= 2:
                    p.wait(sync, f"proj{gq * QT - QT - 1}")
                for b in range(16):
                    p.inc(sync.dma_start_transpose(
                        xlt[gq % 2][:, b * QT * 128:(b + 1) * QT * 128],
                        bass.AP(xln_d, gq * QT * 128 * E + 128 * b,
                                [[E, QT * 128], [1, 128]])), s_xlt2[gq % 2], 16)
                p.ev(f"xltld{gq}", s_xlt2[gq % 2])

            if phases < 2:
                p.wait(sync, f"qkst{NG - 1}")
                p.wait(sync, f"proj{NT - 1}")
                for i2 in range(len(PIECES)):
                    _sy_piece_load(i2)
                    _sy_piece_store(i2)

            if phases >= 2:
                p.wait(sync, f"qkst{NG - 1}")
                if NG >= 2:
                    p.wait(sync, f"qkst{NG - 2}")
                p.wait(sync, f"proj{NT - 1}")
                for h in range(4):
                    p.inc(sync.dma_start_transpose(
                        qT[h][:], bass.AP(q_d, 128 * h, [[512, S], [1, 128]])), s_p2, 16)
                p.inc(sync.dma_start_transpose(
                    kT[:], bass.AP(k_d, 0, [[128, S], [1, 128]])), s_p2, 16)
                p.inc(sync.dma_start_transpose(
                    RqT[:], bass.AP(rq_d, 0, [[128, S], [1, 128]])), s_p2, 16)
                p.inc(sync.dma_start_transpose(
                    RkT[:], bass.AP(rk_d, 0, [[128, S], [1, 128]])), s_p2, 16)
                p.ev("p2ld", s_p2)
                _sy_piece_load(0)
                _sy_piece_load(1)
                _sy_piece_load(2)
                p.wait(sync, f"bnds{NG - 1}")
                p.wait(sync, f"ropeg{NG - 1}")
                p.inc(sync.dma_start(out=bass.AP(bnd_d, 0, [[288, 128], [1, 128]]),
                                     in_=A1_sb[:]), s_bnd, 16)
                p.inc(sync.dma_start(out=bass.AP(bnd_d, 128, [[288, 128], [1, 128]]),
                                     in_=A2_sb[:]), s_bnd, 16)
                p.inc(sync.dma_start(out=bass.AP(bnd_d, 256, [[288, 128], [1, 32]]),
                                     in_=kcol[:]), s_bnd, 16)
                p.ev("bndst", s_bnd)
                p.wait(sync, "bndst")
                rb = ([(256, 32)] + [(32 * h, 32) for h in range(4)]
                      + [(128 + 32 * h, 32) for h in range(4)])
                for i, (off, w) in enumerate(rb):
                    p.inc(sync.dma_start(
                        out=rd_all[:, 32 * i:32 * (i + 1)],
                        in_=bass.AP(bnd_d, off, [[288, 128], [1, w]])), s_bnd, 16)
                p.ev("rdall", s_bnd)
                p.wait(sync, "rdc1")
                p.inc(sync.dma_start(out=bass.AP(bnd2_d, 0, [[16, 128], [1, 9]]),
                                     in_=rdc[:, 0:9]), s_bnd, 16)
                p.ev("rdcst", s_bnd)
                p.wait(sync, "rdcst")
                p.inc(sync.dma_start(out=rd2[0:1, 0:1152],
                                     in_=bass.AP(bnd2_d, 0, [[16, 128], [1, 9]])),
                      s_bnd, 16)
                p.ev("rdld2", s_bnd)
                _sy_piece_store(0)
                p.wait(sync, "nmsb")
                p.inc(sync.dma_start(out=nm_d.ap(), in_=nm_sb[:]), s_nm, 16)
                p.ev("nmst", s_nm)
                p.wait(sync, "nmst")
                for h in range(4):
                    p.inc(sync.dma_start(
                        out=negmaxT[32 * h:32 * h + 1, 0:4096],
                        in_=bass.AP(nm_d, 32 * h * 128, [[4096, 1], [1, 4096]])),
                        s_nm, 16)
                p.ev("nmld", s_nm)
                JPCs = max(NT // 8, 1)
                CHs = S // 4
                pidx_sy = 1
                for J in range(NT // 2):
                    for _ in range(2):
                        if pidx_sy < len(PIECES):
                            _sy_piece_store(pidx_sy)
                            if pidx_sy + 2 < len(PIECES):
                                _sy_piece_load(pidx_sy + 2)
                            pidx_sy += 1
                    p.wait(sync, f"aoev{J}")
                    if CHs >= 256:
                        p.inc(sync.dma_start(
                            out=bass.AP(cc_attn_in,
                                        (J // JPCs) * 512 * CHs + 256 * (J % JPCs),
                                        [[CHs, 128], [128 * CHs, 4], [1, 256]]),
                            in_=ao_sb[J % 4][:]), s_ao2[J % 2], 16)
                    else:
                        for h2 in range(2):
                            p.inc(sync.dma_start(
                                out=bass.AP(cc_attn_in, (2 * J + h2) * 512 * CHs,
                                            [[CHs, 128], [128 * CHs, 4], [1, 128]]),
                                in_=bass.AP(ao_sb[J % 4], 128 * h2,
                                            [[1024, 128], [256, 4], [1, 128]])),
                                s_ao2[J % 2], 16)
                    p.ev(f"aost{J}", s_ao2[J % 2])
                while pidx_sy < len(PIECES):
                    _sy_piece_store(pidx_sy)
                    if pidx_sy + 2 < len(PIECES):
                        _sy_piece_load(pidx_sy + 2)
                    pidx_sy += 1

            if phases >= 3:
                for st in range(NST):
                    if st >= 2:
                        p.wait(sync, f"xtr{st - 2}")
                    p.inc(sync.dma_start(out=xr_sb[st % 2][:],
                                         in_=x_rows.ap()[128 * st:128 * (st + 1), :]),
                          s_xr2[st % 2], 16)
                    p.ev(f"xrld{st}", s_xr2[st % 2])
                p.wait(sync, "wo_ag")
                for half in range(2):
                    if half == 1:
                        p.wait(sync, "wo_half0_done")
                    for ei in range(16):
                        rnk, r0 = ei // 2, 128 * (ei % 2)
                        base = rnk * W_O + r0 * E + 1024 * half
                        p.inc(sync.dma_start(
                            out=wo_half[:, 1024 * ei:1024 * (ei + 1)],
                            in_=bass.AP(cc_wo_out, base, [[E, 128], [1, 1024]])),
                            s_wo, 16)
                    p.ev(f"wo_h{half}", s_wo)
                p.wait(sync, "wg_ag")
                p.wait(sync, "wu_ag")
                for gi, grp in enumerate(GRP):
                    for jj, j in enumerate(grp):
                        rnk, slot = j % 8, j // 8
                        gidx2 = gi * 100 + jj
                        widx = sum(len(g2) for g2 in GRP[:gi]) + jj
                        if widx >= 2:
                            pg, pj = gu_hist[widx - 2]
                            p.wait(sync, f"gumm_{pg}_{pj}")
                        for wsb, ctens in ((wg_sb, cc_wg_out), (wu_sb, cc_wu_out)):
                            base = rnk * W_GU + 128 * slot
                            p.inc(sync.dma_start(
                                out=wsb[widx % 2][:],
                                in_=bass.AP(ctens, base,
                                            [[768, 128], [768 * 128, 16], [1, 128]])),
                                s_gu2[widx % 2], 16)
                        p.ev(f"guld_{gi}_{jj}", s_gu2[widx % 2])
                        gu_hist.append((gi, jj))
                for st in range(NST):
                    for bb2 in range(4):
                        oi = 4 * st + bb2
                        p.wait(sync, f"orow{oi}")
                        p.inc(sync.dma_start(
                            out=bass.AP(o_rows, 128 * st * E + 512 * bb2,
                                        [[E, 128], [1, 512]]),
                            in_=orow[oi % 2][:]), s_or2[oi % 2], 16)
                        p.ev(f"orst{oi}", s_or2[oi % 2])
                if not record:
                    sync.wait_ge(p.sems[s_or2[0]], p.counts[s_or2[0]])
                    sync.wait_ge(p.sems[s_or2[1]], p.counts[s_or2[1]])

            if debug >= 1:
                p.wait(sync, f"qkst{NG - 1}")
                if NG >= 2:
                    p.wait(sync, f"qkst{NG - 2}")
                p.inc(sync.dma_start(out=dbg_q.ap(), in_=q_d.ap()), s_sp, 16)
                p.inc(sync.dma_start(out=dbg_k.ap(), in_=k_d.ap()), s_sp, 16)
                p.inc(sync.dma_start(out=dbg_rq.ap(), in_=rq_d.ap()), s_sp, 16)
                p.inc(sync.dma_start(out=dbg_rk.ap(), in_=rk_d.ap()), s_sp, 16)
                p.wait(sync, "vdone")
                p.inc(sync.dma_start(out=dbg_v.ap(), in_=v_buf[:]), s_sp, 16)
                p.inc(sync.dma_start(out=dbg_xln.ap(), in_=xln_d.ap()), s_sp, 16)
                if phases == 1:
                    p.inc(sync.dma_start(out=dbg_xlt.ap(), in_=xlt[(NG - 1) % 2][:]),
                          s_sp, 16)
                    p.inc(sync.dma_start(out=dbg_wt.ap(), in_=wt_bf[:]), s_sp, 16)
                if phases >= 2:
                    p.wait(sync, f"aost{NT // 2 - 1}")
                    p.wait(sync, f"aost{NT // 2 - 2}")
                    p.inc(sync.dma_start(out=dbg_attn.ap(), in_=cc_attn_in.ap()),
                          s_sp, 16)
                    p.inc(sync.dma_start(out=dbg_nm.ap(), in_=negmaxT[:]), s_sp, 16)
                    p.inc(sync.dma_start(out=dbg_mxc.ap(), in_=mxc[:]), s_sp, 16)
                    p.inc(sync.dma_start(out=dbg_pb.ap(), in_=probsT[0][:, 0:512]), s_sp, 16)
                    p.inc(sync.dma_start(out=dbg_rec.ap(), in_=rec_sb[0:1, 0:512]), s_sp, 16)
                    p.inc(sync.dma_start(out=dbg_rep.ap(), in_=rep_sb[:, 0:512]), s_sp, 16)
                    p.inc(sync.dma_start(out=dbg_ao.ap(), in_=ao_sb[(NT // 2 - 1) % 4][:, 0:512]),
                          s_sp, 16)
                p.wait(sync, "wd_ag")
                p.inc(sync.dma_start(out=bass.AP(dbg_wts, 0, [[1, W_GU]]),
                                     in_=bass.AP(cc_wg_in, 0, [[1, W_GU]])), s_sp, 16)
                p.inc(sync.dma_start(out=bass.AP(dbg_wts, W_GU, [[1, W_GU]]),
                                     in_=bass.AP(cc_wu_in, 0, [[1, W_GU]])), s_sp, 16)
                p.inc(sync.dma_start(out=bass.AP(dbg_wts, OFF_WD, [[1, W_GU]]),
                                     in_=bass.AP(cc_wd_in, 0, [[1, W_GU]])), s_sp, 16)
                p.inc(sync.dma_start(out=bass.AP(dbg_wts, OFF_WO, [[1, W_O]]),
                                     in_=bass.AP(cc_wo_in, 0, [[1, W_O]])), s_sp, 16)
                if not record:
                    sync.wait_ge(p.sems[s_sp], p.counts[s_sp])

        def emit_gpsimd(g):
            p.inc(g.dma_start(out=wt_bf[:],
                              in_=bass.AP(wproj, 0, [[PROJ_OF, 128], [PROJ_OF * 128, 16],
                                                     [1, PROJ_OF]])), s_wt, 16)
            p.ev("wtld", s_wt)
            for i in range(4):
                r0, c0 = (i // 2) * 128, (i % 2) * 1024
                p.inc(g.dma_start(out=wo_bf[:],
                                  in_=wo_sl.ap()[r0:r0 + 128, c0:c0 + 1024]), s_gp, 16)
                p.ev(f"wold{i}", s_gp)
                p.wait(g, f"wold{i}")
                p.inc(g.dma_start(out=bass.AP(cc_wo_in, r0 * E + c0,
                                              [[E, 128], [1, 1024]]),
                                  in_=wo_bf[:]), s_gp, 16)
                p.ev(f"wost{i}", s_gp)
                p.wait(g, f"wost{i}")
            for gg in range(NG):
                p.wait(g, f"ropeg{gg}")
                for ti in range(QT):
                    t = gg * QT + ti
                    p.inc(g.dma_start(out=q_d.ap()[128 * t:128 * (t + 1), :],
                                      in_=qko[ti][:, 0:512]), s_qk2[gg % 2], 16)
                    p.inc(g.dma_start(out=k_d.ap()[128 * t:128 * (t + 1), :],
                                      in_=qko[ti][:, 512:640]), s_qk2[gg % 2], 16)
                    p.inc(g.dma_start(out=rq_d.ap()[128 * t:128 * (t + 1), :],
                                      in_=rqko[ti][:, 0:128]), s_qk2[gg % 2], 16)
                    p.inc(g.dma_start(out=rk_d.ap()[128 * t:128 * (t + 1), :],
                                      in_=rqko[ti][:, 128:256]), s_qk2[gg % 2], 16)
                p.ev(f"qkst{gg}", s_qk2[gg % 2])
            JPC = NT // 4
            CH = S // 4

            def emit_attn_ag(c):
                JPCj = max(NT // 8, 1)
                last_j = min(JPCj * (c + 1), NT // 2)
                p.wait(g, f"aost{last_j - 1}")
                if last_j - 2 >= 0:
                    p.wait(g, f"aost{last_j - 2}")
                p.inc(g.collective_compute(
                    "AllGather", ALU.bypass,
                    replica_groups=[[0, 1, 2, 3], [4, 5, 6, 7]],
                    ins=[bass.AP(cc_attn_in, c * 512 * CH, [[1, 512 * CH]]).opt()],
                    outs=[bass.AP(cc_attn_out, c * 4 * 512 * CH,
                                  [[1, 4 * 512 * CH]]).opt()]), s_cc, 1)
                p.ev(f"ag_a{c}", s_cc)
            if phases >= 2:
                emit_attn_ag(0)
            p.wait(g, f"bst_0_{NP_GU - 1}")
            p.inc(g.collective_compute(
                "AllGather", ALU.bypass, replica_groups=[list(range(8))],
                ins=[cc_wg_in.ap().opt()], outs=[cc_wg_out.ap().opt()]), s_cc, 1)
            p.ev("wg_ag", s_cc)
            p.inc(g.collective_compute(
                "AllGather", ALU.bypass, replica_groups=[list(range(8))],
                ins=[cc_wo_in.ap().opt()], outs=[cc_wo_out.ap().opt()]), s_cc, 1)
            p.ev("wo_ag", s_cc)
            p.wait(g, f"bst_1_{NP_GU - 1}")
            p.inc(g.collective_compute(
                "AllGather", ALU.bypass, replica_groups=[list(range(8))],
                ins=[cc_wu_in.ap().opt()], outs=[cc_wu_out.ap().opt()]), s_cc, 1)
            p.ev("wu_ag", s_cc)
            if phases >= 2:
                emit_attn_ag(1)
            if phases >= 2:
                emit_attn_ag(2)
            p.wait(g, f"bst_2_{NP_D - 1}")
            p.inc(g.collective_compute(
                "AllGather", ALU.bypass, replica_groups=[list(range(8))],
                ins=[cc_wd_in.ap().opt()], outs=[cc_wd_out.ap().opt()]), s_cc, 1)
            p.ev("wd_ag", s_cc)
            if phases >= 2:
                emit_attn_ag(3)
            if phases >= 3:
                p.wait(g, "ag_a3")
                if not record:
                    with g.register("g_r") as g_r, g.register("g_o") as g_o:
                        g.reg_load(g_r, c_rank[0:1, 0:1])
                        g.reg_mul(g_r, g_r, 4 * 512 * CH)
                        for ec in range(16):
                            gg, sub = ec // 4, ec % 4
                            base = gg * 512 * CH + (128 * sub) * CH
                            g.reg_mov(g_o, base)
                            g.reg_add(g_o, g_o, g_r)
                            p.inc(g.dma_start(
                                out=at_sb[:, TOK * ec:TOK * (ec + 1)],
                                in_=bass.AP(cc_attn_out, g_o, [[CH, 128], [1, CH]])),
                                s_at, 16)
                else:
                    for ec in range(16):
                        p.inc(MockInst(), s_at, 16)
                p.ev("at_ld", s_at)
                p.wait(g, "wd_ag")
                for gi, grp in enumerate(GRP):
                    ranks = sorted(set(j % 8 for j in grp))
                    for e8 in range(8):
                        for ri, rnk in enumerate(ranks):
                            slots = [j // 8 for j in grp if j % 8 == rnk]
                            widx = (gi * 8 + e8) * 4 + ri
                            if len(wd_hist) >= 2:
                                pg, pe8, pri = wd_hist[-2]
                                p.wait(g, f"dmr_{pg}_{pe8}_{pri}")
                            base = rnk * W_GU + 256 * e8
                            p.inc(g.dma_start(
                                out=wd_sb[widx % 2][:, 0:256 * len(slots)],
                                in_=bass.AP(cc_wd_out, base,
                                            [[2048, 128], [2048 * 128, len(slots)],
                                             [1, 256]])),
                                s_wd2[widx % 2], 16)
                            p.ev(f"wdld_{gi}_{e8}_{ri}", s_wd2[widx % 2])
                            wd_hist.append((gi, e8, ri))

        def emit_scalar(sc):
            def _sc_piece(gi, m, pc, bp):
                p.wait(sc, f"bidx_{m}_{pc}")
                if gi >= 2:
                    pm, ppc = PIECES[gi - 2][0], PIECES[gi - 2][1]
                    p.wait(sc, f"bacc_{pm}_{ppc}")
                p.inc(sc.copy(out=bd_bf[gi % 2][:, 0:bp],
                              in_=bd_i32[gi % 2][:, 0:bp]), s_sc, 1)
                p.ev(f"bcvt_{m}_{pc}", s_sc)
            sc_bp = iter([(gi2, PIECES[gi2][0], PIECES[gi2][1], PIECES[gi2][2])
                          for gi2 in range(len(PIECES))])
            if phases < 2:
                for nxt in sc_bp:
                    _sc_piece(*nxt)
            p.wait(sc, "consts")
            for gq in range(NG + 1):
                if gq < NG:
                    p.wait(sc, f"statg{gq}")
                    sc.activation(out=st_s[:], in_=st_s[:], func=ACT.Sqrt)
                    sc.drain()
                    p.inc(sc.nop(), s_sc, 1)
                    p.ev(f"sqrtg{gq}", s_sc)
                    p.wait(sc, f"biasg{gq}")
                    for ti in range(QT):
                        t = gq * QT + ti
                        if gq >= 1:
                            p.wait(sc, f"xlnst{t - QT}")
                        p.inc(sc.activation(out=xlnb[ti][:], in_=xbuf[ti][:],
                                            func=ACT.Identity,
                                            bias=st_m[:, ti:ti + 1],
                                            scale=st_s[:, ti:ti + 1]), s_sc, 1)
                        p.ev(f"xln{t}", s_sc)
                if gq >= 1:
                    gp = gq - 1
                    if gp >= 1:
                        p.wait(sc, f"ropeg{gp - 1}")
                        p.wait(sc, f"qkst{gp - 1}")
                    for ti in range(QT):
                        t = gp * QT + ti
                        p.wait(sc, f"proj{t}")
                        bk = (t % 2) * 2
                        p.inc(sc.copy(out=qkf[ti][:, 0:512], in_=PSB[bk][:, 0:512]),
                              s_sc, 1)
                        p.inc(sc.copy(out=qkf[ti][:, 512:640],
                                      in_=PSB[bk + 1][:, 0:128]), s_sc, 1)
                        p.inc(sc.copy(out=v_buf[:, 128 * t:128 * (t + 1)],
                                      in_=PSB[bk + 1][:, 128:256]), s_sc, 1)
                        p.inc(sc.copy(out=rqko[ti][:], in_=PSB[bk + 1][:, 256:512]),
                              s_sc, 1)
                        p.ev(f"evac{t}", s_sc)
                    p.wait(sc, f"recipg{gp}")
                    for ti in range(QT):
                        sc.activation(out=rs_a[ti][:, 0:4], in_=rs_a[ti][:, 0:4],
                                      func=ACT.Sqrt, scale=QSCALE2)
                        sc.activation(out=rs_a[ti][:, 4:5], in_=rs_a[ti][:, 4:5],
                                      func=ACT.Sqrt)
                    sc.drain()
                    p.inc(sc.nop(), s_sc, 1)
                    p.ev(f"rsg{gp}", s_sc)
                    p.wait(sc, f"bnd2{gp}")
                    for ti in range(QT):
                        t = gp * QT + ti
                        sc.activation(out=rs_b[:, 13 * ti:13 * ti + 5],
                                      in_=rs_b[:, 13 * ti:13 * ti + 5], func=ACT.Sqrt)
                        sc.activation(
                            out=bass.AP(A1_sb, t, [[128, 128], [32, 4]]),
                            in_=rs_b[:, 13 * ti + 5:13 * ti + 9], func=ACT.Sqrt)
                        sc.activation(
                            out=bass.AP(A2_sb, t, [[128, 128], [32, 4]]),
                            in_=rs_b[:, 13 * ti + 9:13 * ti + 13], func=ACT.Sqrt)
                    sc.drain()
                    p.inc(sc.nop(), s_sc, 1)
                    p.ev(f"bnds{gp}", s_sc)
            if phases >= 2:
                p.wait(sc, "p2ld")
                p.wait(sc, "svecs")
                sc.activation(out=Rq_s1[:], in_=RqT[:], func=ACT.Identity,
                              scale=s1vec[:])
                sc.activation(out=Rk_s2[:], in_=RkT[:], func=ACT.Identity,
                              scale=s2vec[:])
                sc.drain()
                p.inc(sc.nop(), s_sc, 1)
                p.ev("rscaled", s_sc)
                for _ in range(2):
                    nxt = next(sc_bp, None)
                    if nxt is not None:
                        _sc_piece(*nxt)
            if phases >= 2:
                p.wait(sc, "gsq")
                sc.activation(out=gsa1[:], in_=gsa1[:], func=ACT.Sqrt)
                sc.drain()
                p.inc(sc.nop(), s_sc, 1)
                p.ev("gsabs", s_sc)
                p.wait(sc, "mxbc")
                p.inc(sc.copy(out=mx_bc[:], in_=PSB[1][:, 0:16]), s_sc, 1)
                p.ev("mxbcs", s_sc)
                p.wait(sc, "nmtr")
                p.inc(sc.activation(out=nm_sb[:], in_=PSB[0][:, 0:128],
                                    func=ACT.Identity, scale=-1.0), s_sc, 1)
                p.ev("nmsb", s_sc)
                pidx = 0
                for J in range(NT // 2):
                    for kt in range(2 * J + 2):
                        if kt == 2 * J:
                            p.wait(sc, f"mskA_{J}")
                        elif kt == 2 * J + 1:
                            p.wait(sc, f"mskB_{J}")
                        else:
                            p.wait(sc, f"psB_{J}_{kt}")
                        bb = pidx % 2
                        sc.activation(out=probsT[bb][:, 0:512],
                                      in_=PSB[2 * bb][:], func=ACT.Exp)
                        p.inc(sc.activation(out=probsT[bb][:, 512:1024],
                                            in_=PSB[2 * bb + 1][:],
                                            func=ACT.Exp), s_sc, 1)
                        p.ev(f"exp_{J}_{kt}", s_sc)
                        pidx += 1
                    for _ in range(2):
                        nxt = next(sc_bp, None)
                        if nxt is not None:
                            _sc_piece(*nxt)
                for nxt in sc_bp:
                    _sc_piece(*nxt)
            if phases >= 3:
                for st in range(NST):
                    p.wait(sc, f"xtr{st}")
                    bk0 = 4 * (st % 2)
                    for bb2 in range(4):
                        mm = sc.copy(
                            out=bass.AP(res1, TOK * 4 * bb2 + 128 * st,
                                        [[16 * TOK, 128], [TOK, 4], [1, 128]]),
                            in_=PSB[bk0 + bb2][:, 0:512])
                    p.inc(mm, s_sc, 1)
                    p.ev(f"xev{st}", s_sc)
                p.wait(sc, "lnvr")
                sc.activation(out=rrow[:], in_=vrow[:], func=ACT.Sqrt)
                sc.drain()
                p.inc(sc.nop(), s_sc, 1)
                p.ev("lnrows", s_sc)
                for gi, grp in enumerate(GRP):
                    for jj, j in enumerate(grp):
                        p.wait(sc, f"gumm_{gi}_{jj}")
                        bkg = 4 * (jj % 2)
                        for qi, (qo, ql) in enumerate(QHS):
                            mm = sc.activation(out=gt_sb[jj % 2][:, qo:qo + ql],
                                               in_=PSB[bkg + qi][:, 0:ql],
                                               func=ACT.Silu)
                        p.inc(mm, s_sc, 1)
                        p.ev(f"sg_{gi}_{jj}", s_sc)
                for st in range(NST):
                    p.wait(sc, f"otr{st}")
                    bk0 = 4 * (st % 2)
                    for bb2 in range(4):
                        oi = 4 * st + bb2
                        if oi >= 2:
                            p.wait(sc, f"orst{oi - 2}")
                        p.inc(sc.copy(out=orow[oi % 2][:], in_=PSB[bk0 + bb2][:, 0:512]),
                              s_sc, 1)
                        p.ev(f"orow{oi}", s_sc)

        def emit_vector(v):
            p.wait(v, "consts")
            v.tensor_mul(s1vec[:], c_gs[:], c_s1f[:])
            v.tensor_mul(s2vec[:], c_gs[:], c_s2f[:])
            v.tensor_mul(gsa1[:], c_gsr[:], c_gsr[:])
            v.memset(A_sb[:], 0.0)
            v.memset(A1_sb[:], 0.0)
            v.memset(A2_sb[:], 0.0)
            v.memset(kcol[:], 0.0)
            v.memset(maxes_bf[:], 0.0)
            v.drain()
            p.inc(v.engine_nop(), s_v, 1)
            p.ev("svecs", s_v)
            p.ev("gsq", s_v)
            for gq in range(NG + 1):
                if gq < NG:
                    if gq >= 1:
                        p.wait(v, f"xln{gq * QT - 1}")
                    for ti in range(QT):
                        t = gq * QT + ti
                        p.wait(v, f"xld{t}")
                        v.tensor_mul(st_scr[:], xbuf[ti][:, 0:E // 2],
                                     xbuf[ti][:, 0:E // 2])
                        v.reduce_sum(st_t[:, ti:ti + 1], st_scr[:], axis=AX)
                        v.tensor_mul(st_scr[:], xbuf[ti][:, E // 2:E],
                                     xbuf[ti][:, E // 2:E])
                        v.reduce_sum(st_u[:, ti:ti + 1], st_scr[:], axis=AX)
                        v.reduce_sum(st_m[:, ti:ti + 1], xbuf[ti][:], axis=AX,
                                     negate=True)
                    v.drain()
                    v.tensor_add(st_t[:], st_t[:], st_u[:])
                    v.drain()
                    v.tensor_scalar(out=st_t[:], in0=st_t[:], scalar1=1.0 / E,
                                    scalar2=LN_EPS, op0=ALU.mult, op1=ALU.add)
                    v.tensor_scalar_mul(st_m[:], st_m[:], 1.0 / E)
                    v.drain()
                    v.tensor_mul(st_s[:], st_m[:], st_m[:])
                    v.drain()
                    v.tensor_sub(st_s[:], st_t[:], st_s[:])
                    v.drain()
                    v.reciprocal(st_s[:], st_s[:])
                    v.drain()
                    p.inc(v.engine_nop(), s_v, 1)
                    p.ev(f"statg{gq}", s_v)
                    p.wait(v, f"sqrtg{gq}")
                    v.tensor_mul(st_m[:], st_m[:], st_s[:])
                    v.drain()
                    p.inc(v.engine_nop(), s_v, 1)
                    p.ev(f"biasg{gq}", s_v)
                if gq >= 1:
                    gp = gq - 1
                    for ti in range(QT):
                        t = gp * QT + ti
                        p.wait(v, f"evac{t}")
                        v.tensor_mul(st_scr[:, 0:640], qkf[ti][:, 0:640],
                                     qkf[ti][:, 0:640])
                        for hh in range(5):
                            v.reduce_sum(rs_a[ti][:, hh:hh + 1],
                                         st_scr[:, 128 * hh:128 * (hh + 1)], axis=AX)
                        v.tensor_mul(st_scr[:, 0:256], rqko[ti][:, 0:256],
                                     rqko[ti][:, 0:256])
                        for hh in range(8):
                            v.reduce_sum(rs_b[:, 13 * ti + 5 + hh:13 * ti + 6 + hh],
                                         st_scr[:, 32 * hh:32 * (hh + 1)], axis=AX)
                    v.drain()
                    for ti in range(QT):
                        v.tensor_scalar(out=rs_a[ti][:], in0=rs_a[ti][:],
                                        scalar1=1.0 / D, scalar2=RMS_EPS,
                                        op0=ALU.mult, op1=ALU.add)
                    v.drain()
                    for ti in range(QT):
                        v.reciprocal(rs_a[ti][:], rs_a[ti][:])
                    v.drain()
                    p.inc(v.engine_nop(), s_v, 1)
                    p.ev(f"recipg{gp}", s_v)
                    for ti in range(QT):
                        v.tensor_mul(qkf[ti][:, 0:640], qkf[ti][:, 0:640],
                                     c_wrow[:])
                    v.drain()
                    for ti in range(QT):
                        v.tensor_mul(st_scr[:, 0:640], qkf[ti][:, 0:640],
                                     qkf[ti][:, 0:640])
                        for hh in range(5):
                            v.reduce_sum(rs_b[:, 13 * ti + hh:13 * ti + hh + 1],
                                         st_scr[:, 128 * hh:128 * (hh + 1)], axis=AX)
                    v.drain()
                    p.inc(v.engine_nop(), s_v, 1)
                    p.ev(f"bnd2{gp}", s_v)
                    if gp >= 1:
                        p.wait(v, f"qkst{gp - 1}")
                    p.wait(v, "tblf")
                    p.wait(v, f"rsg{gp}")
                    p.wait(v, f"bnds{gp}")
                    for ti in range(QT):
                        t = gp * QT + ti
                        v.tensor_mul(bass.AP(A_sb, t, [[128, 128], [32, 4]]),
                                     bass.AP(rs_b, 13 * ti, [[52, 128], [1, 4]]),
                                     rs_a[ti][:, 0:4])
                        v.tensor_mul(kcol[:, t:t + 1],
                                     rs_b[:, 13 * ti + 4:13 * ti + 5],
                                     rs_a[ti][:, 4:5])
                    v.drain()
                    for ti in range(QT):
                        t = gp * QT + ti
                        cq_blk = bass.AP(c_full, 128 * t, [[S, 128], [0, 4], [1, 128]])
                        sq_blk = bass.AP(s_full, 128 * t, [[S, 128], [0, 4], [1, 128]])
                        qsh = bass.AP(qkf[ti], 64,
                                      [[640, 128], [128, 4], [-64, 2], [1, 64]])
                        v.tensor_mul(r_u[:], qkf[ti][:, 0:512], cq_blk)
                        v.tensor_mul(qko[ti][:, 0:512], qsh, sq_blk)
                        v.tensor_add(qko[ti][:, 0:512], qko[ti][:, 0:512], r_u[:])
                        v.tensor_mul(qko[ti][:, 0:512], qko[ti][:, 0:512],
                                     bass.AP(rs_a[ti], 0, [[5, 128], [1, 4], [0, 128]]))
                        ck_blk = bass.AP(c_full, 128 * t, [[S, 128], [1, 128]])
                        sk_blk = bass.AP(s_full, 128 * t, [[S, 128], [1, 128]])
                        ksh = bass.AP(qkf[ti], 512 + 64,
                                      [[640, 128], [-64, 2], [1, 64]])
                        v.tensor_mul(r_u[:, 0:128], qkf[ti][:, 512:640], ck_blk)
                        v.tensor_mul(qko[ti][:, 512:640], ksh, sk_blk)
                        v.tensor_add(qko[ti][:, 512:640], qko[ti][:, 512:640],
                                     r_u[:, 0:128])
                        v.tensor_mul(qko[ti][:, 512:640], qko[ti][:, 512:640],
                                     bass.AP(rs_a[ti], 4, [[5, 128], [1, 1], [0, 128]]))
                    v.drain()
                    p.inc(v.engine_nop(), s_v, 1)
                    p.ev(f"ropeg{gp}", s_v)
            p.inc(v.engine_nop(), s_v, 1)
            p.ev("vdone", s_v)

            build_pieces = []
            _prev_st = [None]

            def _emit_piece(gidx, midx, pc):
                mn, srct, base, bp, npc = MATS[midx]
                p.wait(v, f"bcvt_{midx}_{pc}")
                if _prev_st[0] is not None:
                    p.wait(v, _prev_st[0])
                for j in range(NLUT):
                    v.tensor_scalar(out=bd_tmp[:, 0:bp], in0=bd_bf[gidx % 2][:, 0:bp],
                                    scalar1=float(j), scalar2=c_lut[:, j:j + 1],
                                    op0=ALU.is_equal, op1=ALU.mult)
                    if j == 0:
                        v.tensor_copy(bd_acc[:, 0:bp], bd_tmp[:, 0:bp])
                    else:
                        v.tensor_add(bd_acc[:, 0:bp], bd_acc[:, 0:bp],
                                     bd_tmp[:, 0:bp])
                p.inc(v.engine_nop(), s_v, 1)
                p.ev(f"bacc_{midx}_{pc}", s_v)
                _prev_st[0] = f"bst_{midx}_{pc}"

            for gidx2 in range(len(PIECES)):
                build_pieces.append((gidx2, PIECES[gidx2][0], PIECES[gidx2][1]))
            bp_iter = iter(build_pieces)

            if phases >= 2:
                p.wait(v, "consts")
                v.tensor_copy(c_ones32[:], c_ones[:])
                v.memset(negmaxT[:], 0.0)
                v.drain()
                p.inc(v.engine_nop(), s_v, 1)
                p.ev("svecs2", s_v)
                p.wait(v, "gsabs")
                v.tensor_mul(gsa2[:], gsa1[:], c_s2r[:])
                v.drain()
                v.tensor_mul(gsa1[:], gsa1[:], c_s1r[:])
                v.drain()
                p.wait(v, "rdall")
                for i in range(9):
                    v.reduce_max(rdc[:, i:i + 1], rd_all[:, 32 * i:32 * (i + 1)],
                                 axis=AX)
                v.drain()
                p.inc(v.engine_nop(), s_v, 1)
                p.ev("rdc1", s_v)
                p.wait(v, "rdld2")
                for i in range(9):
                    v.reduce_max(maxes_bf[0:1, i:i + 1],
                                 bass.AP(rd2, 128 * i, [[1152, 1], [1, 128]]),
                                 axis=AX)
                v.drain()
                p.inc(v.engine_nop(), s_v, 1)
                p.ev("mxrows", s_v)
                p.wait(v, "mxbcs")
                v.tensor_mul(mxc[:], A_sb[:],
                             bass.AP(mx_bc, 0, [[16, 128], [0, 128]]))
                v.tensor_mul(rep_sb[:, 0:128], A1_sb[:],
                             bass.AP(mx_bc, 5, [[16, 128], [1, 4], [0, 32]]))
                v.drain()
                v.tensor_mul(rep_sb[:, 0:128], rep_sb[:, 0:128],
                             bass.AP(gsa1, 0, [[4, 128], [1, 4], [0, 32]]))
                v.drain()
                v.tensor_add(mxc[:], mxc[:], rep_sb[:, 0:128])
                v.tensor_mul(rep_sb[:, 0:128], A2_sb[:],
                             bass.AP(mx_bc, 1, [[16, 128], [1, 4], [0, 32]]))
                v.drain()
                v.tensor_mul(rep_sb[:, 0:128], rep_sb[:, 0:128],
                             bass.AP(gsa2, 0, [[4, 128], [1, 4], [0, 32]]))
                v.drain()
                v.tensor_add(mxc[:], mxc[:], rep_sb[:, 0:128])
                v.drain()
                p.inc(v.engine_nop(), s_v, 1)
                p.ev("mxcf", s_v)
                for _ in range(2):
                    nxt = next(bp_iter, None)
                    if nxt is not None:
                        _emit_piece(*nxt)
                NJv = NT // 2
                for J in range(NJv):
                    base = J * (2 * J + 2) - J * (J - 1) if False else None
                    pidx_a = sum(2 * jj + 2 for jj in range(J)) + 2 * J
                    p.wait(v, f"psB_{J}_{2 * J}")
                    bba = pidx_a % 2
                    for h in range(4):
                        bk = PSB[2 * bba + h // 2]
                        c0 = 256 * (h % 2)
                        v.tensor_add(bk[:, c0:c0 + 128], bk[:, c0:c0 + 128],
                                     c_mkq[:])
                    v.drain()
                    p.inc(v.engine_nop(), s_v, 1)
                    p.ev(f"mskA_{J}", s_v)
                    p.wait(v, f"psB_{J}_{2 * J + 1}")
                    bbb = (pidx_a + 1) % 2
                    for h in range(4):
                        bk = PSB[2 * bbb + h // 2]
                        c0 = 256 * (h % 2)
                        v.tensor_add(bk[:, c0:c0 + 128], bk[:, c0:c0 + 128],
                                     c_mqk[:])
                        v.tensor_add(bk[:, c0 + 128:c0 + 256],
                                     bk[:, c0 + 128:c0 + 256], c_mkq[:])
                    v.drain()
                    p.inc(v.engine_nop(), s_v, 1)
                    p.ev(f"mskB_{J}", s_v)
                    p.wait(v, f"sums_{J}")
                    v.reciprocal(rec_sb[0:1, 0:512], PSB[6][0:1, 0:512])
                    v.reciprocal(rec_sb[0:1, 512:1024], PSB[7][0:1, 0:512])
                    v.drain()
                    p.inc(v.engine_nop(), s_v, 1)
                    p.ev(f"rec_{J}", s_v)
                    p.wait(v, f"rep_{J}")
                    if J >= 4:
                        p.wait(v, f"aost{J - 4}")
                    v.tensor_copy(rep_sb[:, 0:512], PSB[6][:])
                    v.tensor_copy(rep_sb[:, 512:1024], PSB[7][:])
                    v.drain()
                    for half in range(2):
                        v.tensor_mul(ao_sb[J % 4][:, 512 * half:512 * (half + 1)],
                                     PSB[4 + half][:, 0:512],
                                     rep_sb[:, 512 * half:512 * (half + 1)])
                    v.drain()
                    p.inc(v.engine_nop(), s_v, 1)
                    p.ev(f"aoev{J}", s_v)
                    for _ in range(2):
                        nxt = next(bp_iter, None)
                        if nxt is not None:
                            _emit_piece(*nxt)
                for nxt in bp_iter:
                    _emit_piece(*nxt)
                if phases >= 3:
                    for eo in range(16):
                        p.wait(v, f"womm{eo}")
                        bk0 = 2 * (eo % 2)
                        for qi, (qo, ql) in enumerate(QHS):
                            v.tensor_add(res1[:, TOK * eo + qo:TOK * eo + qo + ql],
                                         res1[:, TOK * eo + qo:TOK * eo + qo + ql],
                                         PSB[bk0 + qi][:, 0:ql])
                        v.drain()
                        p.inc(v.engine_nop(), s_v, 1)
                        p.ev(f"res1e{eo}", s_v)
                    for e in range(16):
                        p.wait(v, f"res1e{e}")
                        if e >= 2:
                            p.wait(v, f"lnmm{e - 2}")
                        v.tensor_mul(sq_sb[e % 2][:], res1[:, TOK * e:TOK * (e + 1)],
                                     res1[:, TOK * e:TOK * (e + 1)])
                        v.drain()
                        p.inc(v.engine_nop(), s_v, 1)
                        p.ev(f"sqv{e}", s_v)
                    p.wait(v, f"lnmm{15}")
                    for qi, (qo, ql) in enumerate(QHS):
                        v.tensor_scalar_mul(mrow[0:1, qo:qo + ql],
                                            PSB[4 + qi][0:1, 0:ql], 1.0 / E)
                        v.tensor_scalar_mul(vrow[0:1, qo:qo + ql],
                                            PSB[6 + qi][0:1, 0:ql], 1.0 / E)
                    v.drain()
                    v.tensor_mul(rrow[:], mrow[:], mrow[:])
                    v.drain()
                    v.tensor_sub(vrow[:], vrow[:], rrow[:])
                    v.drain()
                    v.tensor_scalar_add(vrow[:], vrow[:], LN_EPS)
                    v.drain()
                    v.reciprocal(vrow[:], vrow[:])
                    v.drain()
                    p.inc(v.engine_nop(), s_v, 1)
                    p.ev("lnvr", s_v)
                    p.wait(v, "lnrep")
                    for qi, (qo, ql) in enumerate(QHS):
                        v.tensor_copy(mrep[:, qo:qo + ql], PSB[4 + qi][:, 0:ql])
                        v.tensor_copy(rrep[:, qo:qo + ql], PSB[6 + qi][:, 0:ql])
                    v.drain()
                    p.inc(v.engine_nop(), s_v, 1)
                    p.ev("lnrepv", s_v)
                    for e in range(16):
                        v.tensor_sub(sq_sb[0][:], res1[:, TOK * e:TOK * (e + 1)], mrep[:])
                        v.tensor_mul(xl2[:, TOK * e:TOK * (e + 1)], sq_sb[0][:], rrep[:])
                    v.drain()
                    p.inc(v.engine_nop(), s_v, 1)
                    p.ev("xl2done", s_v)
                    for gi, grp in enumerate(GRP):
                        for jj, j in enumerate(grp):
                            p.wait(v, f"sg_{gi}_{jj}")
                            bkg = 4 * (jj % 2)
                            for qi, (qo, ql) in enumerate(QHS):
                                v.tensor_mul(h_sb[:, TOK * jj + qo:TOK * jj + qo + ql],
                                             PSB[bkg + 2 + qi][:, 0:ql],
                                             gt_sb[jj % 2][:, qo:qo + ql])
                            v.drain()
                            p.inc(v.engine_nop(), s_v, 1)
                            p.ev(f"hv_{gi}_{jj}", s_v)
                        for e8 in range(8):
                            p.wait(v, f"dmm_{gi}_{e8}")
                            bkd = 4 * (e8 % 2)
                            for eol in range(2):
                                eo = 2 * e8 + eol
                                for qi, (qo, ql) in enumerate(QHS):
                                    v.tensor_add(
                                        res1[:, TOK * eo + qo:TOK * eo + qo + ql],
                                        res1[:, TOK * eo + qo:TOK * eo + qo + ql],
                                        PSB[bkd + 2 * eol + qi][:, 0:ql])
                            v.drain()
                            p.inc(v.engine_nop(), s_v, 1)
                            p.ev(f"r2v_{gi}_{e8}", s_v)
                    p.inc(v.engine_nop(), s_v, 1)
                    p.ev("r2done", s_v)
            else:
                for nxt in bp_iter:
                    _emit_piece(*nxt)

        def emit_tensor(te):
            p.wait(te, "wtld")
            for gq in range(NG):
                p.wait(te, f"xltld{gq}")
                for ti in range(QT):
                    t = gq * QT + ti
                    if t >= 2:
                        p.wait(te, f"evac{t - 2}")
                    bk = (t % 2) * 2
                    for b in range(16):
                        xl = xlt[gq % 2][:, b * QT * 128 + ti * 128:
                                         b * QT * 128 + (ti + 1) * 128]
                        te.matmul(PSB[bk][:, 0:512], xl,
                                  wt_bf[:, b * PROJ_OF:b * PROJ_OF + 512],
                                  start=(b == 0), stop=(b == 15))
                        mm = te.matmul(PSB[bk + 1][:, 0:512], xl,
                                       wt_bf[:, b * PROJ_OF + 512:(b + 1) * PROJ_OF],
                                       start=(b == 0), stop=(b == 15))
                    p.inc(mm, s_pe, 1)
                    p.ev(f"proj{t}", s_pe)
            if phases >= 2:
                p.wait(te, "p2ld")
                p.wait(te, "rscaled")
                p.wait(te, f"evac{NT - 1}")
                p.wait(te, f"evac{NT - 2}")
                p.wait(te, "mxrows")
                p.inc(te.matmul(PSB[1][:, 0:16], c_ones[0:1, 0:128],
                                maxes_bf[0:1, 0:16], start=True, stop=True), s_pe, 1)
                p.ev("mxbc", s_pe)
                p.wait(te, "mxcf")
                p.inc(te.transpose(PSB[0][:, 0:128], mxc[:], c_id32[:]), s_pe, 1)
                p.ev("nmtr", s_pe)
                p.wait(te, "nmld")
                p.wait(te, "svecs2")
                p.wait(te, "nmsb")
                hist = []
                NJ = NT // 2
                pairs = [(J, kt) for J in range(NJ) for kt in range(2 * J + 2)]

                def emit_psB(pidx):
                    J, kt = pairs[pidx]
                    js = slice(256 * J, 256 * (J + 1))
                    ks = slice(128 * kt, 128 * (kt + 1))
                    if pidx >= 2:
                        p.wait(te, "exp_%d_%d" % pairs[pidx - 2])
                    bb = pidx % 2
                    for h in range(4):
                        hs = slice(32 * h, 32 * (h + 1))
                        bk = PSB[2 * bb + h // 2]
                        hc = slice(256 * (h % 2), 256 * (h % 2) + 256)
                        te.matmul(bk[:, hc], kT[:, ks], qT[h][:, js],
                                  start=True, stop=False)
                        te.matmul(bk[:, hc], RkT[hs, ks], Rq_s1[hs, js],
                                  start=False, stop=False,
                                  tile_position=(32 * h, 0))
                        te.matmul(bk[:, hc], RqT[hs, ks], Rk_s2[hs, js],
                                  start=False, stop=False,
                                  tile_position=(32 * h, 0))
                        mm = te.matmul(
                            bk[:, hc], c_ones[32 * h:32 * h + 1, 0:128],
                            negmaxT[32 * h:32 * h + 1, js], start=False,
                            stop=True, tile_position=(32 * h, 0))
                    p.inc(mm, s_pe, 1)
                    p.ev(f"psB_{J}_{kt}", s_pe)

                def emit_AV(pidx):
                    J, kt = pairs[pidx]
                    ks = slice(128 * kt, 128 * (kt + 1))
                    bb = pidx % 2
                    st0, st1 = (kt == 0), (kt == 2 * J + 1)
                    p.wait(te, f"exp_{J}_{kt}")
                    if kt == 0 and J >= 1:
                        p.wait(te, f"aoev{J - 1}")
                    for half in range(2):
                        te.matmul(PSB[4 + half][:, 0:512], v_buf[:, ks],
                                  probsT[bb][:, 512 * half:512 * (half + 1)],
                                  start=st0, stop=st1)
                        mm = te.matmul(PSB[6 + half][0:1, 0:512], c_ones[:, 0:1],
                                       probsT[bb][:, 512 * half:512 * (half + 1)],
                                       start=st0, stop=st1)
                    p.inc(mm, s_pe, 1)
                    if st1:
                        p.ev(f"sums_{J}", s_pe)

                def emit_outer(J):
                    p.wait(te, f"rec_{J}")
                    for half in range(2):
                        mm = te.matmul(PSB[6 + half][:, 0:512],
                                       c_ones32[0:1, 0:128],
                                       rec_sb[0:1, 512 * half:512 * (half + 1)],
                                       start=True, stop=True)
                    p.inc(mm, s_pe, 1)
                    p.ev(f"rep_{J}", s_pe)

                pend = []
                for pidx in range(len(pairs)):
                    emit_psB(pidx)
                    if pend and pend[0] <= pidx - 2:
                        emit_outer(pairs[pend.pop(0)][0])
                    if pidx >= 1:
                        emit_AV(pidx - 1)
                        if pairs[pidx - 1][1] == 2 * pairs[pidx - 1][0] + 1:
                            pend.append(pidx - 1)
                emit_AV(len(pairs) - 1)
                pend.append(len(pairs) - 1)
                for pd in pend:
                    emit_outer(pairs[pd][0])
            if phases >= 3:
                p.wait(te, "at_ld")
                p.wait(te, f"aoev{NT // 2 - 1}")
                for st in range(NST):
                    p.wait(te, f"xrld{st}")
                    if st >= 2:
                        p.wait(te, f"xev{st - 2}")
                    bk0 = 4 * (st % 2)
                    for e in range(16):
                        mm = te.transpose(
                            PSB[bk0 + e // 4][:, 128 * (e % 4):128 * (e % 4 + 1)],
                            xr_sb[st % 2][:, 128 * e:128 * (e + 1)], c_id32[:])
                        if e % 4 == 3:
                            p.inc(mm, s_pe, 1)
                    p.ev(f"xtr{st}", s_pe)
                if NST >= 2:
                    p.wait(te, f"xev{NST - 1}")
                    p.wait(te, f"xev{NST - 2}")
                else:
                    p.wait(te, f"xev{NST - 1}")
                for half in range(2):
                    p.wait(te, f"wo_h{half}")
                    for eol in range(8):
                        eo = 8 * half + eol
                        bk0 = 2 * (eo % 2)
                        if eo >= 2:
                            p.wait(te, f"res1e{eo - 2}")
                        for ei in range(16):
                            st2 = (ei == 15)
                            for qi, (qo, ql) in enumerate(QHS):
                                mm = te.matmul(
                                    PSB[bk0 + qi][:, 0:ql],
                                    wo_half[:, 1024 * ei + 128 * eol:1024 * ei + 128 * eol + 128],
                                    at_sb[:, TOK * ei + qo:TOK * ei + qo + ql],
                                    start=(ei == 0), stop=st2)
                        p.inc(mm, s_pe, 1)
                        p.ev(f"womm{eo}", s_pe)
                    p.ev(f"wo_half{half}_done", s_pe)
                for e in range(16):
                    p.wait(te, f"res1e{e}")
                    for qi, (qo, ql) in enumerate(QHS):
                        te.matmul(PSB[4 + qi][0:1, 0:ql], c_ones32[:, 0:1],
                                  res1[:, TOK * e + qo:TOK * e + qo + ql],
                                  start=(e == 0), stop=(e == 15))
                    p.wait(te, f"sqv{e}")
                    for qi, (qo, ql) in enumerate(QHS):
                        mm = te.matmul(PSB[6 + qi][0:1, 0:ql], c_ones32[:, 0:1],
                                       sq_sb[e % 2][:, qo:qo + ql],
                                       start=(e == 0), stop=(e == 15))
                    p.inc(mm, s_pe, 1)
                    p.ev(f"lnmm{e}", s_pe)
                p.wait(te, "lnrows")
                for qi, (qo, ql) in enumerate(QHS):
                    te.matmul(PSB[4 + qi][:, 0:ql], c_ones32[0:1, 0:128],
                              mrow[0:1, qo:qo + ql], start=True, stop=True)
                    mm = te.matmul(PSB[6 + qi][:, 0:ql], c_ones32[0:1, 0:128],
                                   rrow[0:1, qo:qo + ql], start=True, stop=True)
                p.inc(mm, s_pe, 1)
                p.ev("lnrep", s_pe)
                for gi, grp in enumerate(GRP):
                    for jj, j in enumerate(grp):
                        widx = sum(len(g2) for g2 in GRP[:gi]) + jj
                        p.wait(te, f"guld_{gi}_{jj}")
                        if jj == 0:
                            p.wait(te, "xl2done")
                        if gi == 0 and jj == 1:
                            p.wait(te, "lnrepv")
                        if gi == 1 and jj < 2:
                            p.wait(te, f"r2v_{gi - 1}_{6 + jj}")
                        bkg = 4 * (jj % 2)
                        if jj >= 2:
                            p.wait(te, f"hv_{gi}_{jj - 2}")
                        for b in range(16):
                            for qi, (qo, ql) in enumerate(QHS):
                                te.matmul(PSB[bkg + qi][:, 0:ql],
                                          wg_sb[widx % 2][:, 128 * b:128 * (b + 1)],
                                          xl2[:, TOK * b + qo:TOK * b + qo + ql],
                                          start=(b == 0), stop=(b == 15))
                                mm = te.matmul(PSB[bkg + 2 + qi][:, 0:ql],
                                               wu_sb[widx % 2][:, 128 * b:128 * (b + 1)],
                                               xl2[:, TOK * b + qo:TOK * b + qo + ql],
                                               start=(b == 0), stop=(b == 15))
                        p.inc(mm, s_pe, 1)
                        p.ev(f"gumm_{gi}_{jj}", s_pe)
                    ranks = sorted(set(j % 8 for j in grp))
                    for e8 in range(8):
                        bkd = 4 * (e8 % 2)
                        if e8 >= 2:
                            p.wait(te, f"r2v_{gi}_{e8 - 2}")
                        first = True
                        for ri, rnk in enumerate(ranks):
                            slots = [j // 8 for j in grp if j % 8 == rnk]
                            widx = (gi * 8 + e8) * 4 + ri
                            p.wait(te, f"wdld_{gi}_{e8}_{ri}")
                            if e8 == 0 and ri == 0:
                                p.wait(te, f"hv_{gi}_{len(grp) - 1}")
                                p.wait(te, f"hv_{gi}_{len(grp) - 2}")
                            for si, slot in enumerate(slots):
                                j = 8 * slot + rnk
                                jj = grp.index(j)
                                last = (ri == len(ranks) - 1 and si == len(slots) - 1)
                                for eol in range(2):
                                    for qi, (qo, ql) in enumerate(QHS):
                                        mm = te.matmul(
                                            PSB[bkd + 2 * eol + qi][:, 0:ql],
                                            wd_sb[widx % 2][:, 256 * si + 128 * eol:
                                                            256 * si + 128 * (eol + 1)],
                                            h_sb[:, TOK * jj + qo:TOK * jj + qo + ql],
                                            start=first, stop=last)
                                if first:
                                    first = False
                            p.inc(mm, s_pe, 1)
                            p.ev(f"dmr_{gi}_{e8}_{ri}", s_pe)
                        p.ev(f"dmm_{gi}_{e8}", s_pe)
                p.wait(te, "r2done")
                for st in range(NST):
                    if st >= 2:
                        p.wait(te, f"orow{4 * (st - 2) + 3}")
                    bk0 = 4 * (st % 2)
                    for e in range(16):
                        mm = te.transpose(
                            PSB[bk0 + e // 4][:, 128 * (e % 4):128 * (e % 4 + 1)],
                            res1[:, TOK * e + 128 * st:TOK * e + 128 * (st + 1)],
                            c_id32[:])
                        if e % 4 == 3:
                            p.inc(mm, s_pe, 1)
                    p.ev(f"otr{st}", s_pe)
                    p.ev(f"otr{st}", s_pe)

        if record:
            emit_sync(MockEng())
            emit_gpsimd(MockEng())
            emit_scalar(MockEng())
            emit_vector(MockEng())
            emit_tensor(MockEng())
        else:
            with nc.Block() as block:
                @block.sync
                def _(sync):
                    emit_sync(sync)

                @block.gpsimd
                def _(g):
                    emit_gpsimd(g)

                @block.scalar
                def _(sc):
                    emit_scalar(sc)

                @block.vector
                def _(v):
                    emit_vector(v)

                @block.tensor
                def _(te):
                    emit_tensor(te)

    return nc, din_, dout_


def make_in_maps(inputs, S=4096):
    x = np.ascontiguousarray(inputs["L0_attn_x"])
    Wq = np.asarray(inputs["L0_attn_Wq"])
    Wk = np.asarray(inputs["L0_attn_Wk"])
    Wv = np.asarray(inputs["L0_attn_Wv"])
    Wo = np.asarray(inputs["L0_attn_Wo"])
    qn = np.asarray(inputs["L0_attn_q_norm_w"])
    kn = np.asarray(inputs["L0_attn_k_norm_w"])
    Wrq = np.asarray(inputs["L0_attn_Wrq"])
    Wrk = np.asarray(inputs["L0_attn_Wrk"])
    gs = np.asarray(inputs["L0_attn_graph_scale"])
    lut = np.asarray(inputs["L0_ffn_lut"])
    gidx = np.asarray(inputs["L0_ffn_gate_idx"])
    uidx = np.asarray(inputs["L0_ffn_up_idx"])
    didx = np.asarray(inputs["L0_ffn_down_idx"])

    NT = S // 128
    TOK = S // 4
    f32 = np.float32

    inv = 1.0 / BASE_ ** (np.arange(0, D, 2, dtype=np.float64) / D)
    tpos = np.arange(S, dtype=np.float64)
    fr = np.outer(tpos, inv)
    emb = np.concatenate([fr, fr], axis=1)
    cosf = np.cos(emb).astype(f32)
    sinf = np.sin(emb).astype(f32)
    sin_signed = np.concatenate([-sinf[:, :64], sinf[:, 64:]], axis=1)
    cos_pk = cosf.reshape(NT, 128, 128).transpose(1, 0, 2).reshape(128, NT * 128)
    sin_pk = sin_signed.reshape(NT, 128, 128).transpose(1, 0, 2).reshape(128, NT * 128)
    cos_pk = np.ascontiguousarray(cos_pk)
    sin_pk = np.ascontiguousarray(sin_pk)

    mask_qk = np.full((128, 128), NEG_BIG, f32)
    mask_kq = np.where(np.arange(128)[None, :] >= np.arange(128)[:, None], 0.0,
                       NEG_BIG).astype(f32)
    ones128 = np.ones((128, 128), dtype=ml_dtypes.bfloat16)
    ident = np.eye(128, dtype=f32)
    lut_rep = np.broadcast_to(lut.astype(f32), (128, NLUT)).copy()

    in_maps = []
    for c in range(8):
        b, g = c // 4, c % 4
        m = {}
        m["xb"] = np.ascontiguousarray(x[b]).astype(f32)
        m["x_rows"] = np.ascontiguousarray(x[b, TOK * g:TOK * (g + 1)]).astype(f32)
        wsl = np.concatenate([
            Wq[512 * g:512 * (g + 1)],
            Wk[128 * g:128 * (g + 1)],
            Wv[128 * g:128 * (g + 1)],
            Wrq[128 * g:128 * (g + 1)] if g < 3 else np.zeros((128, E), f32),
            Wrk[128 * g:128 * (g + 1)] if g < 3 else np.zeros((128, E), f32),
        ], axis=0).astype(f32)
        m["wproj"] = np.ascontiguousarray(wsl.T)
        m["cos_pk"] = cos_pk
        m["sin_pk"] = sin_pk
        m["wrow"] = np.broadcast_to(np.concatenate(
            [np.tile(qn.astype(f32), 4), kn.astype(f32)]), (128, 640)).copy()
        gsl = gs.astype(f32)[4 * g:4 * (g + 1)] if g < 3 else np.zeros(4, f32)
        m["gs_rep"] = np.ascontiguousarray(np.repeat(gsl, 32).reshape(128, 1))
        m["s1f"] = np.full((128, 1), 0.5 if g == 0 else 1.0, f32)
        m["s2f"] = np.full((128, 1), 0.5 if g == 0 else 0.0, f32)
        m["s1r"] = np.full((128, 4), 0.5 if g == 0 else 1.0, f32)
        m["eps_row"] = np.broadcast_to(
            np.array([LN_EPS, D * RMS_EPS, RMS_EPS], f32), (128, 3)).copy()
        m["s2r"] = np.full((128, 4), 0.5 if g == 0 else 0.0, f32)
        m["gs_row"] = np.broadcast_to(gsl, (128, 4)).copy()
        m["mask_qk"] = mask_qk
        m["mask_kq"] = mask_kq
        m["ones_in"] = ones128
        m["ident_in"] = ident
        m["rank4"] = np.array([[g]], dtype=np.uint32)
        m["lut_rep"] = lut_rep

        def pack_gu(idxmat):
            out = np.zeros((128, SLOTS * 2048), np.int32)
            for s in range(SLOTS):
                j = 8 * s + c
                if j >= ICH:
                    continue
                blk = idxmat[128 * j:128 * (j + 1), :].T
                for bb in range(16):
                    out[:, bb * 768 + 128 * s: bb * 768 + 128 * (s + 1)] = \
                        blk[128 * bb:128 * (bb + 1), :]
            return out

        def pack_d(idxmat):
            out = np.zeros((128, SLOTS * 2048), np.int32)
            for s in range(SLOTS):
                j = 8 * s + c
                if j >= ICH:
                    continue
                out[:, 2048 * s:2048 * (s + 1)] = idxmat[:, 128 * j:128 * (j + 1)].T
            return out

        m["g_idx"] = pack_gu(gidx)
        m["u_idx"] = pack_gu(uidx)
        m["d_idx"] = pack_d(didx)
        m["wo_sl"] = np.ascontiguousarray(Wo.astype(f32)[:, 256 * c:256 * (c + 1)].T)
        in_maps.append(m)
    return in_maps


def kernel(**inputs):
    nc, din_, dout_ = build_program(S=4096, debug=0, phases=3)
    in_maps = make_in_maps(inputs, S=4096)
    res = run_bass_kernel_spmd(nc, in_maps, core_ids=list(range(8)))
    out = np.empty((2, 4096, E), np.float32)
    for c in range(8):
        b, g = c // 4, c % 4
        out[b, 1024 * g:1024 * (g + 1)] = res.results[c]["o_rows"]
    return out
